# version 03
# baseline (speedup 1.0000x reference)
import contextlib
import os as _os
import numpy as np
import concourse.bass as bass
import concourse.mybir as mybir
from concourse.bass_utils import run_bass_kernel_spmd

F32 = mybir.dt.float32
BF16 = mybir.dt.bfloat16
AF = mybir.ActivationFunctionType
ALU = mybir.AluOpType

NCORES = 8
T = 1024
DM = 2048
PROJ = 8512
DS = 4096
CS = 256
NCH = 4
DH = 8192
EPS = 1e-5
KT_IN = DM // 128
S_XBC, S_Z = 34, 32
NEG = -1.0e30

ENGS = ("sync", "tensor", "vector", "scalar", "gpsimd")


NSLOT = 8


class Prog:
    def __init__(self, nc, es):
        self.nc, self.es = nc, es
        self.ops = {e: [] for e in ENGS}
        self.led, self.sems = {}, {}
        self.dma_rr = {"dS": 0, "dG": 0}

    def sem(self, name):
        if name not in self.sems:
            self.sems[name] = self.es.enter_context(self.nc.semaphore(name))
            self.led[name] = 0
        return self.sems[name]

    def op(self, eng, fn, waits=(), inc=None, amt=1):
        tok = None
        if inc is not None:
            self.sem(inc)
            self.led[inc] += amt
            tok = (inc, self.led[inc])
        self.ops[eng].append((fn, inc, amt, [w for w in waits if w is not None]))
        return tok

    def dma_slot(self, q):
        s = self.dma_rr[q]
        self.dma_rr[q] = (s + 1) % NSLOT
        name = f"{q}{s}"
        self.sem(name)
        prev = self.led[name]
        return name, prev

    def dma_toks(self):
        out = []
        for q in ("dS", "dG"):
            for s in range(NSLOT):
                name = f"{q}{s}"
                if name in self.led and self.led[name] > 0:
                    out.append((name, self.led[name]))
        return out

    def barrier(self):
        waits = list(self.dma_toks())
        for s in ("pT", "pV", "pS", "pG", "cc"):
            if s in self.led and self.led[s] > 0:
                waits.append((s, self.led[s]))
        for e in ENGS:
            self.op(e, None, waits=waits)

    def emit(self, block):
        def mk(engname):
            def run(eng):
                wm = {}
                for fn, inc, amt, ws in self.ops[engname]:
                    for (sname, val) in ws:
                        if wm.get(sname, 0) < val:
                            eng.wait_ge(self.sems[sname], val)
                            wm[sname] = val
                    if fn is None:
                        continue
                    instr = fn(eng)
                    if inc is not None:
                        instr.then_inc(self.sems[inc], amt)
            return run
        for e in ENGS:
            getattr(block, e)(mk(e))


def bc_free(ap2d, n_rep):
    return ap2d.unsqueeze(2).broadcast_to(tuple(list(ap2d.shape) + [n_rep]))


def r3(ap2d, h):
    return ap2d.rearrange("p (h q) -> p h q", h=h)


class Ctx:
    pass


def build_program(taps=()):
    nc = bass.Bass("TRN2", target_bir_lowering=False, debug=False)
    es = contextlib.ExitStack()

    def din(name, shape, dtype=F32):
        return nc.dram_tensor(name, shape, dtype, kind="ExternalInput").ap()

    def dint(name, shape, dtype=F32):
        if name in taps:
            return nc.dram_tensor(name, shape, dtype, kind="ExternalOutput").ap()
        return nc.dram_tensor(name, shape, dtype).ap()

    D = {}
    D["x_in"] = din("x", [T, DM])
    D["winT"] = din("winT", [DM, PROJ])
    D["woutT"] = din("woutT", [DS, DM])
    D["upT"] = din("upT", [DM, DH])
    D["downT"] = din("downT", [DH, DM])
    D["convw"] = din("convw", [128, S_XBC * 4])
    D["convb"] = din("convb", [128, S_XBC])
    D["dtb"] = din("dtb", [64, 1])
    D["alog"] = din("alog", [64, 1])
    D["drow"] = din("drow", [1, DS])
    D["rmsw"] = din("rmsw", [128, KT_IN])
    D["normw"] = din("normw", [128, DS // 128])
    D["upsc"] = din("upsc", [128, DH // 128])
    D["dnsc"] = din("dnsc", [128, DM // 128])
    D["osc"] = din("osc", [1, 1])
    D["idn"] = din("idn", [128, 128])
    D["maskneg"] = din("maskneg", [2 * 128, CS])
    D["hm"] = din("hm", [128, 8])
    D["sm"] = din("sm", [128, 8])
    D["out_d"] = nc.dram_tensor("out", [T, DM], F32, kind="ExternalOutput").ap()

    D["xs_d"] = dint("xs_d", [S_Z, 128, T])
    D["sz_d"] = dint("sz_d", [S_Z, 128, T])
    D["dac_d"] = dint("dac_d", [64, T])
    D["decs_d"] = dint("decs_d", [NCH, 64])
    D["cdecs_d"] = dint("cdecs_d", [NCH, 64])
    D["decrow_d"] = dint("decrow_d", [NCH, DS])
    D["cdecrow_d"] = dint("cdecrow_d", [NCH, DS])
    D["wrowx_d"] = dint("wrowx_d", [NCORES, DS])
    D["yb_d"] = dint("yb_d", [2 * NCH, 128, DS])
    D["l_d"] = dint("l_d", [NCH, 128, DS])
    D["wrow_d"] = dint("wrow_d", [8, 64])
    D["x1_d"] = dint("x1_d", [T // 128, 128, DM])
    D["vnh_d"] = dint("vnh_d", [DS // 128, 128, T], BF16)
    D["vnl_d"] = dint("vnl_d", [DS // 128, 128, T], BF16)
    D["attn_d"] = dint("attn_d", [DM // 128, 128, T])
    D["cch_in"] = dint("cch_in", [128, S_XBC * 3])
    D["cch_out"] = dint("cch_out", [NCORES * 128, S_XBC * 3])
    D["ccs_in"] = dint("ccs_in", [128, DS + 64])
    D["ccs_out"] = dint("ccs_out", [NCORES * 128, DS + 64])

    nph = int(_os.environ.get("KDBG_PHASES", "8"))
    if _os.environ.get("KDBG_P6B", "1") == "0":
        nph = min(nph, 7)
    with es:
        p = Prog(nc, es)
        blk = es.enter_context(nc.Block())
        c = Ctx()
        c.nc, c.p, c.es, c.D = nc, p, es, D
        _setup(c)
        phases = [_p0, _p1, _p2, _p3, _p4, _p5, _p6, _p7]
        for i, ph in enumerate(phases):
            if i < nph:
                ph(c)
        if nph < 8:
            for attr in ("es67", "es45", "es01"):
                st_ = getattr(c, attr, None)
                if st_:
                    st_.close()
            zb = es.enter_context(nc.sbuf_tensor("zout", [128, DM], F32))
            tzb = c.vmemset(zb[:], 0.0)
            p.barrier()
            touts = [c.dma("gpsimd", D["out_d"][i * 128:(i + 1) * 128, :], zb[:], waits=[tzb])
                     for i in range(8)]
            p.op("sync", None, waits=touts)
            p.op("gpsimd", None, waits=touts)
        p.emit(blk)
    return nc


def _mk_helpers(c):
    p = c.p

    def dma(eng, dst, src, waits=(), q=None, slow=False):
        q = q or ("dS" if eng == "sync" else "dG")
        kw = {"allow_slow_non_contiguous": True} if slow else {}
        sem, prev = p.dma_slot(q)
        ws = list(waits) + ([(sem, prev)] if prev > 0 else [])
        return p.op(eng, (lambda d_, s_, kw_: lambda e: e.dma_start(out=d_, in_=s_, **kw_))(dst, src, kw),
                    waits=ws, inc=sem, amt=16)

    def act(out, in_, func, waits=(), scale=None, bias=None, accum=None):
        kw = {}
        if scale is not None:
            kw["scale"] = scale
        if bias is not None:
            kw["bias"] = bias
        if accum is not None:
            kw["accum_out"] = accum
        return p.op("scalar", (lambda o_, i_, f_, kw_: lambda e: e.activation(
            out=o_, in_=i_, func=f_, **kw_))(out, in_, func, kw), waits=waits, inc="pS")

    def sign(out, in_, waits=()):
        return p.op("scalar", (lambda o_, i_: lambda e: e.sign(out=o_, in_=i_))(out, in_),
                    waits=waits, inc="pS")

    def vtt(out, in0, in1, opr, waits=()):
        return p.op("vector", (lambda o_, a_, b_, op_: lambda e: e.tensor_tensor(
            out=o_, in0=a_, in1=b_, op=op_))(out, in0, in1, opr), waits=waits, inc="pV")

    def vts(out, in0, s1, op0, waits=(), s2=None, op1=None):
        def f(o_, a_, s1_, op0_, s2_, op1_):
            def g(e):
                if s2_ is None:
                    return e.tensor_scalar(out=o_, in0=a_, scalar1=s1_, scalar2=None, op0=op0_)
                return e.tensor_scalar(out=o_, in0=a_, scalar1=s1_, scalar2=s2_, op0=op0_, op1=op1_)
            return g
        return p.op("vector", f(out, in0, s1, op0, s2, op1), waits=waits, inc="pV")

    def vstt(out, in0, scal, op0, in1, op1, waits=()):
        return p.op("vector", (lambda o_, a_, s_, o0, b_, o1: lambda e: e.scalar_tensor_tensor(
            out=o_, in0=a_, scalar=s_, in1=b_, op0=o0, op1=o1))(out, in0, scal, op0, in1, op1),
            waits=waits, inc="pV")

    def vcopy(out, in_, waits=()):
        return p.op("vector", (lambda o_, i_: lambda e: e.tensor_copy(out=o_, in_=i_))(out, in_),
                    waits=waits, inc="pV")

    def vmemset(ap, val, waits=()):
        return p.op("vector", (lambda a_, v_: lambda e: e.memset(a_, v_))(ap, val),
                    waits=waits, inc="pV")

    def vrecip(out, in_, waits=()):
        return p.op("vector", (lambda o_, i_: lambda e: e.reciprocal(out=o_, in_=i_))(out, in_),
                    waits=waits, inc="pV")

    def vreduce(out, in_, waits=()):
        return p.op("vector", (lambda o_, i_: lambda e: e.reduce_sum(
            out=o_, in_=i_, axis=mybir.AxisListType.X))(out, in_), waits=waits, inc="pV")

    def vscan(out, d0, d1, waits=()):
        return p.op("vector", (lambda o_, a_, b_: lambda e: e.tensor_tensor_scan(
            out=o_, data0=a_, data1=b_, initial=0.0, op0=ALU.add, op1=ALU.bypass))(out, d0, d1),
            waits=waits, inc="pV")

    def mm(out, lhsT, rhs, start, stop, waits=(), inc=None):
        return p.op("tensor", (lambda o_, l_, r_, st_, sp_: lambda e: e.matmul(
            out=o_, lhsT=l_, rhs=r_, start=st_, stop=sp_))(out, lhsT, rhs, start, stop),
            waits=waits, inc=inc)

    def tr(out, in_, waits=(), inc="pT"):
        np_ = in_.shape[0]
        return p.op("tensor", (lambda o_, i_, n_: lambda e: e.transpose(
            out=o_, in_=i_, identity=c.idn_s[0:n_, 0:n_]))(out, in_, np_), waits=waits, inc=inc)

    c.dma, c.act, c.sign, c.vtt, c.vts, c.vstt, c.vcopy = dma, act, sign, vtt, vts, vstt, vcopy
    c.vmemset, c.vrecip, c.vreduce, c.vscan, c.mm, c.tr = vmemset, vrecip, vreduce, vscan, mm, tr


def _setup(c):
    nc, p, es, D = c.nc, c.p, c.es, c.D
    sb = lambda name, shape, dtype=F32: es.enter_context(nc.sbuf_tensor(name, shape, dtype))
    c.sb = sb
    c.idn_s = sb("idn_s", [128, 128])
    c.convw_s = sb("convw_s", [128, S_XBC * 4])
    c.convb_s = sb("convb_s", [128, S_XBC])
    c.dtb_s = sb("dtb_s", [64, 1])
    c.alog_s = sb("alog_s", [64, 1])
    c.rmsw_s = sb("rmsw_s", [128, KT_IN])
    c.normw_s = sb("normw_s", [128, DS // 128])
    c.upsc_s = sb("upsc_s", [128, DH // 128])
    c.dnsc_s = sb("dnsc_s", [128, DM // 128])
    c.osc_s = sb("osc_s", [128, 1])
    c.maskneg_s = sb("maskneg_s", [128, 2 * CS])
    c.hm_s = sb("hm_s", [128, 8])
    c.sm_s = sb("sm_s", [128, 8])
    c.zeros64 = sb("zeros64", [128, 64])
    c.B_fm = sb("B_fm", [128, T])
    c.C_fm = sb("C_fm", [128, T])
    c.B_tm = sb("B_tm", [128, NCH * 2 * 128])
    c.dtraw = sb("dtraw", [64, T])
    c.dactm = sb("dactm", [128, 512])
    c.dttm = sb("dttm", [128, 512])
    c.wtm = sb("wtm", [128, 512])
    c.etm = sb("etm", [128, 512])
    c.decs_s = sb("decs_s", [64, NCH])
    c.cdecs_s = sb("cdecs_s", [64, NCH])
    c.ld_s = sb("ld_s", [64, 1])
    c.dn2 = sb("dn2", [128, DM // 128])
    _mk_helpers(c)
    tC = []
    for dst, src in [
        (c.idn_s[:], D["idn"]), (c.convw_s[:], D["convw"]), (c.convb_s[:], D["convb"]),
        (c.dtb_s[:], D["dtb"]), (c.alog_s[:], D["alog"]), (c.rmsw_s[:], D["rmsw"]),
        (c.normw_s[:], D["normw"]), (c.upsc_s[:], D["upsc"]), (c.dnsc_s[:], D["dnsc"]),
        (c.hm_s[:], D["hm"]), (c.sm_s[:], D["sm"]),
        (c.maskneg_s.ap().rearrange("p (i s) -> p i s", i=2),
         D["maskneg"].rearrange("(i p) s -> p i s", i=2)),
        (c.osc_s[:], D["osc"].broadcast_to((128, 1))),
    ]:
        tC.append(c.dma("sync", dst, src))
    c.tC = tC
    c.tz = c.vmemset(c.zeros64[:], 0.0)
    c.eps1 = c.sb("eps1", [128, 1])
    c.teps = c.vmemset(c.eps1[:], EPS)
    c.tdn2 = c.vts(c.dn2[:], c.dnsc_s[:], c.osc_s[:, 0:1], ALU.mult, waits=[tC[8], tC[12]])


def _p0(c):
    nc, p, es, D = c.nc, c.p, c.es, c.D
    c.es01 = contextlib.ExitStack()
    c.hh = c.es01.enter_context(nc.sbuf_tensor("hh", [128, KT_IN * T], BF16))
    c.hl = c.es01.enter_context(nc.sbuf_tensor("hl", [128, KT_IN * T], BF16))
    es0 = contextlib.ExitStack()
    sb0 = lambda n, s, d=F32: es0.enter_context(nc.sbuf_tensor(n, s, d))
    x_tm = sb0("x_tm", [128, 8 * DM])
    sq_scr = sb0("sq_scr", [128, DM])
    sstat = sb0("sstat", [128, 8])
    rs_t = sb0("rs_t", [128, 8])
    hfb = sb0("hfb", [128, 2 * T])
    p0ps = [es0.enter_context(nc.psum_tensor(f"p0ps{i}", [128, 512], F32)) for i in range(2)]

    t_xn = []
    prev_sq = None
    for i in range(8):
        xi = x_tm[:, i * DM:(i + 1) * DM]
        t_ld = c.dma("sync", xi, D["x_in"][i * 128:(i + 1) * 128, :])
        t_sq = c.act(sq_scr[:], xi, AF.Square, waits=[t_ld, prev_sq], accum=sstat[:, i:i + 1])
        prev_sq = t_sq
        t_ln = c.act(rs_t[:, i:i + 1], sstat[:, i:i + 1], AF.Ln, waits=[t_sq, c.teps],
                     scale=1.0 / DM, bias=c.eps1[:])
        t_ex = c.act(rs_t[:, i:i + 1], rs_t[:, i:i + 1], AF.Exp, waits=[t_ln], scale=0.5)
        t_rc = c.vrecip(rs_t[:, i:i + 1], rs_t[:, i:i + 1], waits=[t_ex])
        t_xn.append(c.act(xi, xi, AF.Identity, waits=[t_rc], scale=rs_t[:, i:i + 1]))

    t_hsplit = None
    ps_free = [c.tz, c.tz]
    for kt in range(KT_IN):
        jb = kt % 2
        hfv = hfb[:, jb * T:(jb + 1) * T]
        evs = []
        for g in range(2):
            pb = p0ps[g]
            last = None
            for ii in range(4):
                i = g * 4 + ii
                last = c.tr(pb[:, ii * 128:(ii + 1) * 128],
                            x_tm[:, i * DM + kt * 128: i * DM + kt * 128 + 128],
                            waits=[t_xn[i], c.tC[0], ps_free[g]])
            evs.append(c.vcopy(hfv[:, g * 512:(g + 1) * 512], pb[:], waits=[last, t_hsplit]))
            ps_free[g] = evs[-1]
        hhs = c.hh[:, kt * T:(kt + 1) * T]
        hls = c.hl[:, kt * T:(kt + 1) * T]
        t_hi = c.vts(hhs, hfv, c.rmsw_s[:, kt:kt + 1], ALU.mult, waits=evs + [c.tC[5]])
        t_hsplit = c.vstt(hls, hfv, c.rmsw_s[:, kt:kt + 1], ALU.mult, hhs, ALU.subtract,
                          waits=[t_hi])
    c.t_hsplit = t_hsplit
    p.barrier()
    es0.close()


def _p1(c):
    nc, p, es, D = c.nc, c.p, c.es, c.D
    es1 = contextlib.ExitStack()
    sb1 = lambda n, s, d=F32: es1.enter_context(nc.sbuf_tensor(n, s, d))
    wf = [sb1(f"wf{j}", [128, KT_IN * 128]) for j in range(2)]
    wh = [sb1(f"wh{j}", [128, KT_IN * 128], BF16) for j in range(2)]
    wl = [sb1(f"wl{j}", [128, KT_IN * 128], BF16) for j in range(2)]
    xbcb = [sb1(f"xbcb{j}", [128, 1028]) for j in range(2)]
    xc2 = [sb1(f"xc2{j}", [128, 2 * T]) for j in range(2)]
    xsb = [sb1(f"xsb{j}", [128, T]) for j in range(2)]
    szb = [sb1(f"szb{j}", [128, T]) for j in range(2)]
    stub6 = sb1("stub6", [128, S_XBC * 6])
    ext9 = sb1("ext9", [128, S_XBC * 9])
    halo = sb1("halo", [128, S_XBC * 3])
    xcfix = sb1("xcfix", [128, S_XBC * 3])
    xsfix = sb1("xsfix", [128, S_XBC * 3])
    hblk = sb1("hblk", [128, S_XBC * 3])
    mmps = [es1.enter_context(nc.psum_tensor(f"mmps{i}", [128, 512], F32)) for i in range(4)]

    winT_re = D["winT"].rearrange("(k p) o -> p k o", p=128)

    txz = [c.vmemset(xbcb[j][:, 0:4], 0.0) for j in range(2)]
    t_wld, t_whl, t_wsp, t_mm, t_ev = {}, {}, {}, {}, {}
    t_conv, t_silu, t_sp, t_l3s, t_stubs = {}, {}, {}, {}, {}
    t_ccH = None

    def meta(i):
        if i < S_XBC:
            return 4096 + 128 * i, 128, "xbc"
        if i < S_XBC + S_Z:
            return 128 * (i - S_XBC), 128, "z"
        return 8448, 64, "dt"

    NSTRIP = S_XBC + S_Z + 1
    for i in range(NSTRIP):
        o0, ow, kind = meta(i)
        j = i % 2
        nk = KT_IN * ow
        t_wld[i] = c.dma("sync", wf[j][:, :nk].rearrange("p (k o) -> p k o", o=ow),
                         winT_re[:, :, o0:o0 + ow],
                         waits=[t_whl.get(i - 2), t_wsp.get(i - 2)])
        t_whl[i] = c.dma("gpsimd", wh[j][:, :nk], wf[j][:, :nk],
                         waits=[t_wld[i], t_mm.get((i - 2, 1))])
        t_wsp[i] = c.vtt(wl[j][:, :nk], wf[j][:, :nk], wh[j][:, :nk], ALU.subtract,
                         waits=[t_whl[i], t_mm.get((i - 2, 1))])
        for r in range(2):
            pb = mmps[(2 * i + r) % 4]
            pbv = pb[:, :] if ow == 128 else pb[0:ow, :]
            w0 = [t_wsp[i], c.t_hsplit, t_ev.get((i - 2, r))]
            nmm = 0
            for kt in range(KT_IN):
                lw_h = wh[j][:, kt * ow:(kt + 1) * ow]
                lw_l = wl[j][:, kt * ow:(kt + 1) * ow]
                rh_h = c.hh[:, kt * T + r * 512: kt * T + r * 512 + 512]
                rh_l = c.hl[:, kt * T + r * 512: kt * T + r * 512 + 512]
                for (lw, rh) in ((lw_h, rh_h), (lw_h, rh_l), (lw_l, rh_h)):
                    nmm += 1
                    t_last = c.mm(pbv, lw, rh, start=(nmm == 1), stop=(nmm == 3 * KT_IN),
                                  waits=w0 if nmm == 1 else (),
                                  inc="pT" if nmm == 3 * KT_IN else None)
            t_mm[(i, r)] = t_last
            if kind == "xbc":
                t_ev[(i, r)] = c.act(xbcb[j][:, 4 + r * 512: 4 + r * 512 + 512], pb[:], AF.Copy,
                                     waits=[t_mm[(i, r)], t_conv.get(i - 2), txz[j],
                                            t_l3s.get(i - 2), t_stubs.get(i - 2)])
            elif kind == "z":
                t_ev[(i, r)] = c.act(szb[j][:, r * 512:(r + 1) * 512], pb[:], AF.Silu,
                                     waits=[t_mm[(i, r)], t_sp.get(i - 2)])
            else:
                t_ev[(i, r)] = c.act(c.dtraw[:, r * 512:(r + 1) * 512], pb[0:64, :], AF.Copy,
                                     waits=[t_mm[(i, r)]])
        evb = [t_ev[(i, 0)], t_ev[(i, 1)]]
        if kind == "xbc":
            t = i
            t_l3 = c.dma("gpsimd", D["cch_in"][:, 3 * t:3 * t + 3], xbcb[j][:, 1025:1028],
                         waits=evb)
            t_l3s[t] = t_l3
            t_stub = c.act(stub6[:, 6 * t:6 * t + 6], xbcb[j][:, 4:10], AF.Copy, waits=evb)
            t_stubs[t] = t_stub
            cw = lambda k: c.convw_s[:, 4 * t + k:4 * t + k + 1]
            xa, xb_ = xc2[j][:, 0:T], xc2[j][:, T:2 * T]
            c1 = c.vts(xa, xbcb[j][:, 1:1 + T], cw(0), ALU.mult,
                       waits=evb + [t_silu.get(t - 2), c.tC[1], c.tC[2]],
                       s2=c.convb_s[:, t:t + 1], op1=ALU.add)
            c2 = c.vstt(xb_, xbcb[j][:, 2:2 + T], cw(1), ALU.mult, xa, ALU.add, waits=[c1])
            c3 = c.vstt(xa, xbcb[j][:, 3:3 + T], cw(2), ALU.mult, xb_, ALU.add, waits=[c2])
            c4 = c.vstt(xb_, xbcb[j][:, 4:4 + T], cw(3), ALU.mult, xa, ALU.add, waits=[c3])
            t_conv[t] = c4
            t_silu[t] = c.act(xsb[j][:], xb_, AF.Silu, waits=[c4, t_sp.get(("x", t - 2)),
                                                              t_stub])
            if t < 32:
                t_sp[("x", t)] = c.dma("gpsimd", D["xs_d"][t], xsb[j][:], waits=[t_silu[t]])
            elif t == 32:
                c.tB = c.vcopy(c.B_fm[:], xsb[j][:], waits=[t_silu[t]])
                t_sp[("x", t)] = c.tB
            else:
                c.tCc = c.vcopy(c.C_fm[:], xsb[j][:], waits=[t_silu[t]])
                t_sp[("x", t)] = c.tCc
            if t == S_XBC - 1:
                t_ccH = p.op("gpsimd", lambda e: e.collective_compute(
                    "AllGather", ALU.bypass, replica_groups=[list(range(NCORES))],
                    ins=[D["cch_in"].opt()], outs=[D["cch_out"].opt()]),
                    waits=list(t_l3s.values()), inc="cc")
        elif kind == "z":
            t_sp[i] = c.dma("gpsimd", D["sz_d"][i - S_XBC], szb[j][:], waits=evb)

    t_acc = None
    for jr in range(NCORES):
        t_hb = c.dma("sync", hblk[:], D["cch_out"][jr * 128:(jr + 1) * 128, :],
                     waits=[t_ccH, t_acc])
        if jr == 0:
            t_acc = c.vts(halo[:], hblk[:], c.hm_s[:, 0:1], ALU.mult, waits=[t_hb, c.tC[9]])
        else:
            t_acc = c.vstt(halo[:], hblk[:], c.hm_s[:, jr:jr + 1], ALU.mult, halo[:], ALU.add,
                           waits=[t_hb])
    t_ext = []
    prev = None
    for t in range(S_XBC):
        e1 = c.act(ext9[:, 9 * t:9 * t + 3], halo[:, 3 * t:3 * t + 3], AF.Copy,
                   waits=[t_acc, prev])
        e2 = c.act(ext9[:, 9 * t + 3:9 * t + 9], stub6[:, 6 * t:6 * t + 6], AF.Copy, waits=[e1])
        prev = e2
        t_ext.append(e2)
    t_fix = None
    for t in range(S_XBC):
        cw = lambda k: c.convw_s[:, 4 * t + k:4 * t + k + 1]
        dst = xcfix[:, 3 * t:3 * t + 3]
        f1 = c.vts(dst, ext9[:, 9 * t:9 * t + 3], cw(0), ALU.mult, waits=[t_ext[t], t_fix],
                   s2=c.convb_s[:, t:t + 1], op1=ALU.add)
        f2 = c.vstt(dst, ext9[:, 9 * t + 1:9 * t + 4], cw(1), ALU.mult, dst, ALU.add, waits=[f1])
        f3 = c.vstt(dst, ext9[:, 9 * t + 2:9 * t + 5], cw(2), ALU.mult, dst, ALU.add, waits=[f2])
        t_fix = c.vstt(dst, ext9[:, 9 * t + 3:9 * t + 6], cw(3), ALU.mult, dst, ALU.add,
                       waits=[f3])
    t_sfix = c.act(xsfix[:], xcfix[:], AF.Silu, waits=[t_fix])
    for t in range(32):
        c.dma("gpsimd", D["xs_d"][t][:, 0:3], xsfix[:, 3 * t:3 * t + 3],
              waits=[t_sfix, t_sp[("x", t)]])
    c.vcopy(c.B_fm[:, 0:3], xsfix[:, 96:99], waits=[t_sfix, c.tB])
    c.vcopy(c.C_fm[:, 0:3], xsfix[:, 99:102], waits=[t_sfix, c.tCc])

    p.barrier()
    es1.close()
    c.es01.close()


def _p2(c):
    nc, p, es, D = c.nc, c.p, c.es, c.D
    es2 = contextlib.ExitStack()
    sb2 = lambda n, s, d=F32: es2.enter_context(nc.sbuf_tensor(n, s, d))
    a64 = sb2("a64", [64, T])
    b64 = sb2("b64", [64, T])
    dt_s = sb2("dt_s", [64, T])
    dac_s = sb2("dac_s", [64, T])
    wst_s = sb2("wst_s", [64, T])
    Aneg = sb2("Aneg", [64, 1])
    tmp64 = sb2("tmp64", [64, 1])
    trp2 = [es2.enter_context(nc.psum_tensor(f"trp2{i}", [128, 512], F32)) for i in range(2)]

    ta = c.act(a64[:], c.dtraw[:], AF.Abs, waits=[c.tC[3]], bias=c.dtb_s[:])
    te = c.act(a64[:], a64[:], AF.Exp, waits=[ta], scale=-1.0)
    tl = c.act(a64[:], a64[:], AF.Ln, waits=[te], bias=1.0)
    trl = c.act(b64[:], c.dtraw[:], AF.Relu, waits=[tl], bias=c.dtb_s[:])
    tdt = c.vtt(dt_s[:], a64[:], b64[:], ALU.add, waits=[tl, trl])
    tA1 = c.act(Aneg[:], c.alog_s[:], AF.Exp, waits=[c.tC[4], trl])
    tA2 = c.vts(Aneg[:], Aneg[:], -1.0, ALU.mult, waits=[tA1])
    tdA = c.vts(a64[:], dt_s[:], Aneg[:, 0:1], ALU.mult, waits=[tdt, tA2])
    t_scan = []
    for ch in range(NCH):
        sl = slice(ch * CS, (ch + 1) * CS)
        t_scan.append(c.vscan(dac_s[:, sl], a64[:, sl], a64[:, sl], waits=[tdA]))
    t_dec = []
    for ch in range(NCH):
        t_dec.append(c.act(c.decs_s[:, ch:ch + 1], dac_s[:, ch * CS + CS - 1:ch * CS + CS],
                           AF.Exp, waits=[t_scan[ch]]))
    t_cd = [c.vmemset(c.cdecs_s[:, 0:1], 1.0)]
    for ch in range(1, NCH):
        t_cd.append(c.vtt(c.cdecs_s[:, ch:ch + 1], c.cdecs_s[:, ch - 1:ch],
                          c.decs_s[:, ch - 1:ch], ALU.mult, waits=[t_cd[-1], t_dec[ch - 1]]))
    tld1 = c.vtt(c.ld_s[:], dac_s[:, CS - 1:CS], dac_s[:, 2 * CS - 1:2 * CS], ALU.add,
                 waits=[t_scan[0], t_scan[1]])
    tld2 = c.vtt(tmp64[:], dac_s[:, 3 * CS - 1:3 * CS], dac_s[:, 4 * CS - 1:4 * CS], ALU.add,
                 waits=[t_scan[2], t_scan[3]])
    c.tld = c.vtt(c.ld_s[:], c.ld_s[:], tmp64[:], ALU.add, waits=[tld1, tld2])
    t_w = []
    for ch in range(NCH):
        sl = slice(ch * CS, (ch + 1) * CS)
        tw1 = c.act(wst_s[:, sl], dac_s[:, sl], AF.Exp, waits=[t_scan[ch], trl],
                    scale=-1.0, bias=dac_s[:, ch * CS + CS - 1:ch * CS + CS])
        t_w.append(c.vtt(wst_s[:, sl], wst_s[:, sl], dt_s[:, sl], ALU.mult, waits=[tw1, tdt]))
    c.t_dacsp = c.dma("gpsimd", D["dac_d"], dac_s[:], waits=t_scan)
    c.t_decsp = c.dma("gpsimd", D["decs_d"].rearrange("c h -> h c"), c.decs_s[:], waits=t_dec, slow=True)
    c.t_cdsp = c.dma("gpsimd", D["cdecs_d"].rearrange("c h -> h c"), c.cdecs_s[:], waits=t_cd, slow=True)
    tp_free = [c.tz, c.tz]
    t_tm = {}
    for (src, dst, name, wts) in ((dac_s, c.dactm, "dac", t_scan), (dt_s, c.dttm, "dt", [tdt]),
                                  (wst_s, c.wtm, "w", t_w)):
        for cb in range(8):
            g = cb % 2
            ttr = c.tr(trp2[g][0:128, 0:64], src[:, cb * 128:(cb + 1) * 128],
                       waits=list(wts) + [tp_free[g]])
            tev = c.vcopy(dst[:, cb * 64:(cb + 1) * 64], trp2[g][0:128, 0:64], waits=[ttr])
            tp_free[g] = tev
            t_tm[(name, cb)] = tev
    for cb in range(8):
        g = cb % 2
        ttr = c.tr(trp2[g][:, 0:128], c.B_fm[:, cb * 128:(cb + 1) * 128],
                   waits=[c.tB, tp_free[g]])
        tev = c.vcopy(c.B_tm[:, cb * 128:(cb + 1) * 128], trp2[g][:, 0:128], waits=[ttr])
        tp_free[g] = tev
        t_tm[("B", cb)] = tev
    c.t_etm = c.act(c.etm[:], c.dactm[:], AF.Exp, waits=[t_tm[("dac", cb)] for cb in range(8)])
    c.t_tm = t_tm
    decx = sb2("decx", [NCH, DS])
    dect = sb2("dect", [NCH, 64])
    tex = None
    t_sp_prev = None
    for (srct, dst, wts) in ((c.decs_s, D["decrow_d"], t_dec), (c.cdecs_s, D["cdecrow_d"], t_cd)):
        ttr = c.tr(trp2[0][0:NCH, 0:64], srct[:], waits=list(wts) + [tp_free[0]])
        tev = c.vcopy(dect[:], trp2[0][0:NCH, 0:64], waits=[ttr, tex])
        tp_free[0] = tev
        tex = c.vcopy(decx.ap().rearrange("c (h q) -> c h q", h=64),
                      dect.ap().unsqueeze(2).broadcast_to((NCH, 64, 64)),
                      waits=[tev, t_sp_prev])
        t_sp_prev = c.dma("gpsimd", dst, decx[:], waits=[tex])
        if srct is c.decs_s:
            c.t_decrow = t_sp_prev
        else:
            c.t_cdecrow = t_sp_prev
    p.barrier()
    es2.close()


def _p3(c):
    nc, p, es, D = c.nc, c.p, c.es, c.D
    es3 = contextlib.ExitStack()
    sb3 = lambda n, s, d=F32: es3.enter_context(nc.sbuf_tensor(n, s, d))
    Lbuf = sb3("Lbuf", [128, DS])
    decb = sb3("decb", [128, DS])
    drow_s = sb3("drow_s", [128, DS])
    Ltmp = sb3("Ltmp", [128, 512])
    ybb = sb3("ybb", [128, 512])
    tdrow = c.dma("sync", drow_s[:], D["drow"].broadcast_to((128, DS)))
    t_Lup = c.vmemset(Lbuf[:], 0.0)
    t_ybsp = None

    for ch in range(NCH):
        esA = contextlib.ExitStack()
        xsc = esA.enter_context(nc.sbuf_tensor(f"xsc_{ch}", [128, S_Z * CS], F32))
        xsw = esA.enter_context(nc.sbuf_tensor(f"xsw_{ch}", [128, 2 * DS], F32))
        cbt = esA.enter_context(nc.sbuf_tensor(f"cbt_{ch}", [128, 2 * CS], F32))
        trp3 = [esA.enter_context(nc.psum_tensor(f"trp3{i}_{ch}", [128, 512], F32))
                for i in range(2)]
        t_xsc = c.dma("sync", xsc.ap().rearrange("p (s t) -> p s t", s=S_Z),
                      D["xs_d"][:, :, ch * CS:(ch + 1) * CS].rearrange("s p t -> p s t"))
        pfree = [None, None]
        ev_last = {0: None, 1: None}
        for lt in range(2):
            for sg in range(8):
                g = sg % 2
                pb = trp3[g]
                tlast = None
                for k in range(4):
                    st = sg * 4 + k
                    tlast = c.tr(pb[:, k * 128:(k + 1) * 128],
                                 xsc[:, st * CS + lt * 128: st * CS + lt * 128 + 128],
                                 waits=[t_xsc, pfree[g]])
                wslice = c.wtm[:, (2 * ch + lt) * 64 + sg * 8:(2 * ch + lt) * 64 + sg * 8 + 8]
                e2 = c.vtt(r3(xsw[:, lt * DS + sg * 512: lt * DS + (sg + 1) * 512], 8),
                           r3(pb[:], 8), bc_free(wslice, 64), ALU.mult,
                           waits=[tlast, c.t_tm[("w", 2 * ch + lt)]])
                ev_last[lt] = e2
                pfree[g] = e2
        t_cbt = []
        for stile in range(2):
            pb = trp3[stile]
            tcb = c.mm(pb[:, 0:CS], c.B_fm[:, ch * CS + stile * 128: ch * CS + stile * 128 + 128],
                       c.C_fm[:, ch * CS:(ch + 1) * CS], start=True, stop=True,
                       waits=[pfree[0], pfree[1], c.tB, c.tCc], inc="pT")
            t_cbt.append(c.vcopy(cbt[:, stile * CS:(stile + 1) * CS], pb[:, 0:CS], waits=[tcb]))
        if ch >= 1:
            t_Lsp = c.dma("gpsimd", D["l_d"][ch], Lbuf[:], waits=[t_Lup])
        else:
            t_Lsp = None
        t_decb = c.dma("gpsimd", decb[:],
                       D["decrow_d"][ch:ch + 1, :].broadcast_to((128, DS)),
                       waits=[c.t_decrow, t_Lup])
        t_Lnb = []
        prev_ev = t_cbt[1]
        for nb in range(8):
            lastm = None
            for lt in range(2):
                lastm = c.mm(trp3[0][:, 0:512],
                             c.B_tm[:, (2 * ch + lt) * 128:(2 * ch + lt) * 128 + 128],
                             xsw[:, lt * DS + nb * 512: lt * DS + (nb + 1) * 512],
                             start=(lt == 0), stop=(lt == 1),
                             waits=[ev_last[0], ev_last[1], c.t_tm[("B", 2 * ch)],
                                    c.t_tm[("B", 2 * ch + 1)], prev_ev] if lt == 0 else (),
                             inc="pT" if lt == 1 else None)
            sl = slice(nb * 512, (nb + 1) * 512)
            tm1 = c.vtt(Ltmp[:], Lbuf[:, sl], decb[:, sl], ALU.mult,
                        waits=[t_decb, t_Lup, t_Lsp, lastm])
            prev_ev = c.vtt(Lbuf[:, sl], Ltmp[:], trp3[0][:, 0:512], ALU.add, waits=[tm1])
            t_Lnb.append(prev_ev)
        t_Lup = t_Lnb[-1]
        p.barrier()
        esA.close()

        esB = contextlib.ExitStack()
        xsc2 = esB.enter_context(nc.sbuf_tensor(f"xsc2_{ch}", [128, S_Z * CS], F32))
        xsdt = esB.enter_context(nc.sbuf_tensor(f"xsdt_{ch}", [128, 2 * DS], F32))
        gtm = esB.enter_context(nc.sbuf_tensor(f"gtm_{ch}", [128, 2 * DS], F32))
        cbt2 = esB.enter_context(nc.sbuf_tensor(f"cbt2_{ch}", [128, 2 * CS], F32))
        trp3b = [esB.enter_context(nc.psum_tensor(f"trp3b{i}_{ch}", [128, 512], F32))
                 for i in range(2)]
        t_xsc2 = c.dma("sync", xsc2.ap().rearrange("p (s t) -> p s t", s=S_Z),
                       D["xs_d"][:, :, ch * CS:(ch + 1) * CS].rearrange("s p t -> p s t"))
        t_cbt2 = []
        for stile in range(2):
            pb = trp3b[stile]
            tcb = c.mm(pb[:, 0:CS], c.B_fm[:, ch * CS + stile * 128: ch * CS + stile * 128 + 128],
                       c.C_fm[:, ch * CS:(ch + 1) * CS], start=True, stop=True,
                       waits=[c.tB, c.tCc], inc="pT")
            t_cbt2.append(c.vcopy(cbt2[:, stile * CS:(stile + 1) * CS], pb[:, 0:CS], waits=[tcb]))
        pfree = [t_cbt2[0], t_cbt2[1]]
        ev_last = {0: None, 1: None}
        for lt in range(2):
            for sg in range(8):
                g = sg % 2
                pb = trp3b[g]
                tlast = None
                for k in range(4):
                    st = sg * 4 + k
                    tlast = c.tr(pb[:, k * 128:(k + 1) * 128],
                                 xsc2[:, st * CS + lt * 128: st * CS + lt * 128 + 128],
                                 waits=[t_xsc2, pfree[g]])
                dtsl = c.dttm[:, (2 * ch + lt) * 64 + sg * 8:(2 * ch + lt) * 64 + sg * 8 + 8]
                e1 = c.vtt(r3(xsdt[:, lt * DS + sg * 512: lt * DS + (sg + 1) * 512], 8),
                           r3(pb[:], 8), bc_free(dtsl, 64), ALU.mult,
                           waits=[tlast, c.t_tm[("dt", 2 * ch + lt)]])
                e3 = c.vtt(gtm[:, lt * DS + sg * 512: lt * DS + (sg + 1) * 512],
                           pb[:], drow_s[:, sg * 512:(sg + 1) * 512], ALU.mult,
                           waits=[e1, tdrow])
                ev_last[lt] = e3
                pfree[g] = e3
        p.barrier()
        esB2 = contextlib.ExitStack()
        mb = [esB2.enter_context(nc.sbuf_tensor(f"mb{st}_{ch}", [128, 8 * CS], F32))
              for st in range(2)]
        bct = esB2.enter_context(nc.sbuf_tensor(f"bct_{ch}", [128, 8 * CS], F32))
        yps = [esB2.enter_context(nc.psum_tensor(f"yps{i}_{ch}", [128, 512], F32))
               for i in range(2)]
        t_ydiag = {}
        t_mbprev = None
        t_bct = None
        for q in range(8):
            t_bct = c.dma("gpsimd", r3(bct[:], 8),
                          D["dac_d"][q * 8:(q + 1) * 8, ch * CS:(ch + 1) * CS]
                          .unsqueeze(0).broadcast_to((128, 8, CS)),
                          waits=[c.t_dacsp, t_mbprev, t_bct])
            t_mb = {}
            for stile in range(2):
                prev = t_mbprev
                for hq in range(8):
                    h = q * 8 + hq
                    col = (2 * ch + stile) * 64 + h
                    prev = c.vstt(mb[stile][:, hq * CS:(hq + 1) * CS],
                                  bct[:, hq * CS:(hq + 1) * CS],
                                  c.dactm[:, col:col + 1], ALU.subtract,
                                  c.maskneg_s[:, stile * CS:(stile + 1) * CS], ALU.add,
                                  waits=[t_bct, c.t_tm[("dac", 2 * ch + stile)], prev])
                texp = c.act(mb[stile][:], mb[stile][:], AF.Exp, waits=[prev])
                cbt_b = cbt2[:, stile * CS:(stile + 1) * CS].unsqueeze(1).broadcast_to((128, 8, CS))
                t_mb[stile] = c.vtt(r3(mb[stile][:], 8), r3(mb[stile][:], 8), cbt_b,
                                    ALU.mult, waits=[texp, t_cbt2[stile]])
            t_mbprev = t_mb[1]
            for lt in range(2):
                pb = yps[(2 * q + lt) % 2]
                lastm = None
                first = True
                for hq in range(8):
                    h = q * 8 + hq
                    for stile in range(2):
                        lastm = c.mm(pb[:, hq * 64:(hq + 1) * 64],
                                     mb[stile][:, hq * CS + lt * 128: hq * CS + lt * 128 + 128],
                                     xsdt[:, stile * DS + h * 64: stile * DS + h * 64 + 64],
                                     start=(first and stile == 0) if False else (hq == 0 and stile == 0),
                                     stop=(hq == 7 and stile == 1),
                                     waits=[t_mb[0], t_mb[1], ev_last[0], ev_last[1],
                                            t_ydiag.get((q - 1, lt))] if first else (),
                                     inc="pT" if (hq == 7 and stile == 1) else None)
                        first = False
                tev = c.vtt(ybb[:], pb[:],
                            gtm[:, lt * DS + q * 512: lt * DS + (q + 1) * 512], ALU.add,
                            waits=[lastm, ev_last[lt], t_ybsp])
                t_ybsp = c.dma("gpsimd", D["yb_d"][2 * ch + lt][:, q * 512:(q + 1) * 512],
                               ybb[:], waits=[tev])
                t_ydiag[(q, lt)] = t_ybsp
        p.barrier()
        esB2.close()
        esB.close()

    tzp = c.dma("gpsimd", D["ccs_in"][:, DS:DS + 64], c.zeros64[:], waits=[c.tz])
    tldsp = c.dma("gpsimd", D["ccs_in"][0:64, DS:DS + 1], c.ld_s[:], waits=[c.tld, tzp],
                  slow=True)
    tslsp = c.dma("gpsimd", D["ccs_in"][:, 0:DS], Lbuf[:], waits=[t_Lup])
    c.t_ccS = c.p.op("gpsimd", lambda e: e.collective_compute(
        "AllGather", ALU.bypass, replica_groups=[list(range(NCORES))],
        ins=[D["ccs_in"].opt()], outs=[D["ccs_out"].opt()]),
        waits=[tldsp, tslsp], inc="cc")
    p.barrier()
    es3.close()


def _p4(c):
    nc, p, es, D = c.nc, c.p, c.es, c.D
    c.es45 = contextlib.ExitStack()
    carry = [c.es45.enter_context(nc.sbuf_tensor(f"carry{i}", [128, DS], F32)) for i in range(2)]
    es4 = contextlib.ExitStack()
    sb4 = lambda n, s, d=F32: es4.enter_context(nc.sbuf_tensor(n, s, d))
    gbuf = [sb4(f"gbuf{i}", [128, DS]) for i in range(2)]
    wbb = [sb4(f"wbb{i}", [128, DS]) for i in range(2)]
    ldcols = sb4("ldcols", [128, NCORES])
    wrow = sb4("wrow", [64, NCORES])
    smask = sb4("smask", [64, 1])
    cum = sb4("cum", [64, 1])
    arg = sb4("arg", [64, 1])

    W = DS + 64
    t_ldc = c.dma("sync", ldcols.ap().unsqueeze(2),
                  D["ccs_out"].rearrange("(j p) f -> p j f", p=128)[:, :, DS:DS + 1],
                  waits=[c.t_ccS], slow=True)
    t_sm = c.vts(smask[:], ldcols[0:64, 0:1], c.sm_s[0:64, 0:1], ALU.mult,
                 waits=[t_ldc, c.tC[10]])
    for j in range(1, NCORES):
        t_sm = c.vstt(smask[:], ldcols[0:64, j:j + 1], c.sm_s[0:64, j:j + 1], ALU.mult,
                      smask[:], ALU.add, waits=[t_sm])
    t_cum = None
    t_w = []
    for j in range(NCORES):
        if j == 0:
            t_cum = c.vcopy(cum[:], ldcols[0:64, 0:1], waits=[t_ldc, t_sm])
        else:
            t_cum = c.vtt(cum[:], cum[:], ldcols[0:64, j:j + 1], ALU.add, waits=[t_cum])
        ta = c.vtt(arg[:], smask[:], cum[:], ALU.subtract, waits=[t_cum, t_sm])
        tmin = c.vts(arg[:], arg[:], 0.0, ALU.min, waits=[ta])
        tex = c.act(arg[:], arg[:], AF.Exp, waits=[tmin])
        t_w.append(c.vtt(wrow[:, j:j + 1], arg[:], c.sm_s[0:64, j:j + 1], ALU.mult,
                         waits=[tex]))
    wrt = sb4("wrt", [NCORES, 64])
    wrx = sb4("wrx", [NCORES, DS])
    trp4 = es4.enter_context(nc.psum_tensor("trp4", [128, 512], F32))
    ttr = c.tr(trp4[0:NCORES, 0:64], wrow[:], waits=t_w)
    tev = c.vcopy(wrt[:], trp4[0:NCORES, 0:64], waits=[ttr])
    tex = c.vcopy(wrx.ap().rearrange("j (h q) -> j h q", h=64),
                  wrt.ap().unsqueeze(2).broadcast_to((NCORES, 64, 64)), waits=[tev])
    t_wsp = c.dma("gpsimd", D["wrowx_d"], wrx[:], waits=[tex])
    t_cz = c.vmemset(carry[0][:], 0.0)
    cur = 0
    t_prev = t_cz
    for j in range(NCORES):
        t_g = c.dma("sync", gbuf[j % 2][:],
                    D["ccs_out"][j * 128:(j + 1) * 128, 0:DS], waits=[c.t_ccS, t_prev])
        t_wb = c.dma("gpsimd", wbb[j % 2][:],
                     D["wrowx_d"][j:j + 1, :].broadcast_to((128, DS)),
                     waits=[t_wsp, t_prev])
        t_mu = c.vtt(gbuf[j % 2][:], gbuf[j % 2][:], wbb[j % 2][:], ALU.mult,
                     waits=[t_g, t_wb])
        t_prev = c.vtt(carry[1 - cur][:], carry[cur][:], gbuf[j % 2][:], ALU.add,
                       waits=[t_mu, t_prev])
        cur = 1 - cur
    c.carry = carry[cur]
    c.t_carry = t_prev
    p.barrier()
    es4.close()


def _p5(c):
    nc, p, es, D = c.nc, c.p, c.es, c.D
    es5 = contextlib.ExitStack()
    sb5 = lambda n, s, d=F32: es5.enter_context(nc.sbuf_tensor(n, s, d))
    psb = sb5("psb", [128, DS])
    cdb = sb5("cdb", [128, DS])
    lbuf = sb5("lbuf", [128, DS])
    szc = sb5("szc", [128, S_Z * 128])
    sztm = sb5("sztm", [128, DS])
    ybl = sb5("ybl", [128, DS])
    vv = sb5("vv", [128, DS])
    scr1 = sb5("scr1", [128, 512])
    vnh_b = sb5("vnh_b", [128, S_Z * 128], BF16)
    vnl_b = sb5("vnl_b", [128, S_Z * 128], BF16)
    ssq = sb5("ssq", [128, 1])
    rs5 = sb5("rs5", [128, 1])
    trp5 = [es5.enter_context(nc.psum_tensor(f"trp5{i}", [128, 512], F32)) for i in range(2)]
    yq5 = [es5.enter_context(nc.psum_tensor(f"yq5{i}", [128, 512], F32)) for i in range(2)]

    t_vn = {}
    t_done = [None, None]
    for ch in range(NCH):
        if ch == 0:
            ps_ap = c.carry[:]
            t_ps = c.t_carry
        else:
            t_cdb = c.dma("gpsimd", cdb[:],
                          D["cdecrow_d"][ch:ch + 1, :].broadcast_to((128, DS)),
                          waits=[c.t_cdecrow, t_done[0]])
            t_lb = c.dma("sync", lbuf[:], D["l_d"][ch], waits=[t_done[0]])
            tp1 = c.vtt(psb[:], c.carry[:], cdb[:], ALU.mult,
                        waits=[t_cdb, c.t_carry, t_done[0]])
            t_ps = c.vtt(psb[:], psb[:], lbuf[:], ALU.add, waits=[tp1, t_lb])
            ps_ap = psb[:]
        for lt in range(2):
            c0 = ch * CS + lt * 128
            t_szc = c.dma("sync", szc.ap().rearrange("p (s t) -> p s t", s=S_Z),
                          D["sz_d"][:, :, c0:c0 + 128].rearrange("s p t -> p s t"),
                          waits=[t_done[1], t_done[0]])
            pfree = [t_done[0], t_done[1]]
            t_sz = None
            for sg in range(8):
                g = sg % 2
                pb = trp5[g]
                tlast = None
                for k in range(4):
                    st = sg * 4 + k
                    tlast = c.tr(pb[:, k * 128:(k + 1) * 128],
                                 szc[:, st * 128:(st + 1) * 128],
                                 waits=[t_szc, pfree[g]])
                t_sz = c.vcopy(sztm[:, sg * 512:(sg + 1) * 512], pb[:], waits=[tlast])
                pfree[g] = t_sz
            t_ybl = c.dma("sync", ybl[:], D["yb_d"][2 * ch + lt], waits=[t_done[0]])
            t_v = None
            for q in range(8):
                pb = yq5[q % 2]
                tmm = c.mm(pb[:], c.C_fm[:, ch * CS + lt * 128: ch * CS + lt * 128 + 128],
                           ps_ap[:, q * 512:(q + 1) * 512],
                           start=True, stop=True,
                           waits=[t_ps, c.tCc, t_v], inc="pT")
                esl = c.etm[:, (2 * ch + lt) * 64 + q * 8:(2 * ch + lt) * 64 + q * 8 + 8]
                tv1 = c.vtt(r3(scr1[:], 8), r3(pb[:], 8), bc_free(esl, 64), ALU.mult,
                            waits=[tmm, c.t_etm])
                tv2 = c.vtt(scr1[:], scr1[:], ybl[:, q * 512:(q + 1) * 512], ALU.add,
                            waits=[tv1, t_ybl])
                t_v = c.vtt(vv[:, q * 512:(q + 1) * 512], scr1[:],
                            sztm[:, q * 512:(q + 1) * 512], ALU.mult,
                            waits=[tv2, t_sz])
            t_sq = c.act(ybl[:], vv[:], AF.Square, waits=[t_v], accum=ssq[:])
            t_ln = c.act(rs5[:], ssq[:], AF.Ln, waits=[t_sq, c.teps], scale=1.0 / DS,
                         bias=c.eps1[:])
            t_ex = c.act(rs5[:], rs5[:], AF.Exp, waits=[t_ln], scale=0.5)
            t_rc = c.vrecip(rs5[:], rs5[:], waits=[t_ex])
            t_ap = c.act(vv[:], vv[:], AF.Identity, waits=[t_rc], scale=rs5[:])
            t_hi = t_ap
            t_lo = None
            pfree2 = [t_ap, t_ap]
            for sg in range(8):
                g = sg % 2
                pb = trp5[g]
                tlast = None
                for k in range(4):
                    st = sg * 4 + k
                    tlast = c.tr(pb[:, k * 128:(k + 1) * 128],
                                 vv[:, st * 128:(st + 1) * 128],
                                 waits=[t_ap, pfree2[g]])
                for k in range(4):
                    st = sg * 4 + k
                    dsth = vnh_b[:, st * 128:(st + 1) * 128]
                    dstl = vnl_b[:, st * 128:(st + 1) * 128]
                    t_hi = c.vts(dsth, pb[:, k * 128:(k + 1) * 128],
                                 c.normw_s[:, st:st + 1], ALU.mult,
                                 waits=[tlast, c.tC[6], t_vn.get((ch, lt - 1, "h")),
                                        t_vn.get((ch - 1, 1, "h"))])
                    t_lo = c.vstt(dstl, pb[:, k * 128:(k + 1) * 128],
                                  c.normw_s[:, st:st + 1], ALU.mult, dsth, ALU.subtract,
                                  waits=[t_hi, t_vn.get((ch, lt - 1, "l")),
                                         t_vn.get((ch - 1, 1, "l"))])
                pfree2[g] = t_lo
            t_vn[(ch, lt, "h")] = c.dma("gpsimd",
                                        D["vnh_d"][:, :, c0:c0 + 128].rearrange("s p t -> p s t"),
                                        vnh_b.ap().rearrange("p (s t) -> p s t", s=S_Z),
                                        waits=[t_lo])
            t_vn[(ch, lt, "l")] = c.dma("gpsimd",
                                        D["vnl_d"][:, :, c0:c0 + 128].rearrange("s p t -> p s t"),
                                        vnl_b.ap().rearrange("p (s t) -> p s t", s=S_Z),
                                        waits=[t_lo])
            t_done[0] = t_vn[(ch, lt, "h")]
            t_done[1] = t_vn[(ch, lt, "l")]
    c.t_vnsp = [t_vn[(NCH - 1, 1, "h")], t_vn[(NCH - 1, 1, "l")]]
    p.barrier()
    es5.close()
    c.es45.close()


def _p6(c):
    nc, p, es, D = c.nc, c.p, c.es, c.D
    es6 = contextlib.ExitStack()
    sb6 = lambda n, s, d=F32: es6.enter_context(nc.sbuf_tensor(n, s, d))
    vnh = sb6("vnh", [128, 32 * T], BF16)
    vnl = sb6("vnl", [128, 32 * T], BF16)
    wfo = sb6("wfo", [128, 32 * 128])
    who = sb6("who", [128, 32 * 128], BF16)
    wlo = sb6("wlo", [128, 32 * 128], BF16)
    attnb = [sb6(f"attnb{i}", [128, T]) for i in range(2)]
    mm6 = [es6.enter_context(nc.psum_tensor(f"mm6{i}", [128, 512], F32)) for i in range(4)]

    t_vh = c.dma("sync", vnh.ap().rearrange("p (s t) -> p s t", s=32),
                 D["vnh_d"].rearrange("s p t -> p s t"), waits=[c.t_vnsp[0]])
    t_vl = c.dma("sync", vnl.ap().rearrange("p (s t) -> p s t", s=32),
                 D["vnl_d"].rearrange("s p t -> p s t"), waits=[c.t_vnsp[1]])
    woutT_re = D["woutT"].rearrange("(k p) o -> p k o", p=128)
    t_wld = t_whl = t_wsp = None
    t_ev = {}
    t_mm = {}
    t_spA = {}
    for st in range(16):
        t_wld = c.dma("sync", wfo.ap().rearrange("p (k o) -> p k o", o=128),
                      woutT_re[:, :, st * 128:(st + 1) * 128],
                      waits=[t_whl, t_wsp])
        t_whl = c.dma("gpsimd", who[:], wfo[:], waits=[t_wld, t_mm.get((st - 1, 1))])
        t_wsp = c.vtt(wlo[:], wfo[:], who[:], ALU.subtract,
                      waits=[t_whl, t_mm.get((st - 1, 1))])
        for r in range(2):
            pb = mm6[(2 * st + r) % 4]
            nmm = 0
            for kt in range(32):
                lw_h = who[:, kt * 128:(kt + 1) * 128]
                lw_l = wlo[:, kt * 128:(kt + 1) * 128]
                rh_h = vnh[:, kt * T + r * 512: kt * T + r * 512 + 512]
                rh_l = vnl[:, kt * T + r * 512: kt * T + r * 512 + 512]
                for (lw, rh) in ((lw_h, rh_h), (lw_h, rh_l), (lw_l, rh_h)):
                    nmm += 1
                    tm_ = c.mm(pb[:], lw, rh, start=(nmm == 1), stop=(nmm == 96),
                               waits=[t_wsp, t_vh, t_vl, t_ev.get((st - 2, r))]
                               if nmm == 1 else (),
                               inc="pT" if nmm == 96 else None)
            t_mm[(st, r)] = tm_
            t_ev[(st, r)] = c.act(attnb[st % 2][:, r * 512:(r + 1) * 512], pb[:], AF.Copy,
                                  waits=[tm_, t_spA.get(st - 2)])
        t_spA[st] = c.dma("gpsimd", D["attn_d"][st], attnb[st % 2][:],
                          waits=[t_ev[(st, 0)], t_ev[(st, 1)]])
    p.barrier()
    es6.close()
    if _os.environ.get("KDBG_P6B", "1") == "0":
        c.es67 = None
        c.t_x1 = None
        return

    c.es67 = contextlib.ExitStack()
    x1h = c.es67.enter_context(nc.sbuf_tensor("x1h", [128, KT_IN * T], BF16))
    x1l = c.es67.enter_context(nc.sbuf_tensor("x1l", [128, KT_IN * T], BF16))
    es6b = contextlib.ExitStack()
    sb6b = lambda n, s, d=F32: es6b.enter_context(nc.sbuf_tensor(n, s, d))
    attnf = sb6b("attnf", [128, 16 * T])
    u6 = sb6b("u6", [128, DM])
    x1pt = sb6b("x1pt", [128, DM])
    xpt = sb6b("xpt", [128, DM])
    sq6 = sb6b("sq6", [128, DM])
    st1 = sb6b("st1", [128, 1])
    st2 = sb6b("st2", [128, 1])
    mtile = sb6b("mtile", [128, 1])
    nbias = sb6b("nbias", [128, 1])
    trp6 = [es6b.enter_context(nc.psum_tensor(f"trp6{i}", [128, 512], F32)) for i in range(2)]

    t_af = c.dma("sync", attnf.ap().rearrange("p (s t) -> p s t", s=16),
                 D["attn_d"].rearrange("s p t -> p s t"))
    t_prev = None
    t_x1sp = None
    for i in range(8):
        t_x = c.dma("sync", xpt[:], D["x_in"][i * 128:(i + 1) * 128, :], waits=[t_prev])
        t_u = []
        pf = [t_prev, t_prev]
        for g in range(4):
            pb = trp6[g % 2]
            tlast = None
            for k in range(4):
                st = g * 4 + k
                tlast = c.tr(pb[:, k * 128:(k + 1) * 128],
                             attnf[:, st * T + i * 128: st * T + i * 128 + 128],
                             waits=[t_af, pf[g % 2]])
            t_u.append(c.vstt(u6[:, g * 512:(g + 1) * 512], xpt[:, g * 512:(g + 1) * 512],
                              2.0, ALU.mult, pb[:], ALU.add, waits=[tlast, t_x, t_prev]))
            pf[g % 2] = t_u[-1]
        lvl = _os.environ.get("KDBG_P6B", "9")
        if lvl == "1":
            t_prev = t_u[-1]
            continue
        t_r = c.vreduce(st1[:], u6[:], waits=t_u)
        t_q = c.act(sq6[:], u6[:], AF.Square, waits=t_u + [t_prev], accum=st2[:])
        t_m = c.vts(mtile[:], st1[:], 1.0 / DM, ALU.mult, waits=[t_r])
        t_msq = c.vtt(st1[:], mtile[:], mtile[:], ALU.mult, waits=[t_m, t_r])
        if lvl == "2":
            t_prev = t_msq
            continue
        t_var = c.vstt(st1[:], st2[:], 1.0 / DM, ALU.mult, st1[:], ALU.subtract,
                       waits=[t_q, t_msq])
        t_ln = c.act(st1[:], st1[:], AF.Ln, waits=[t_var, c.teps], bias=c.eps1[:])
        t_ex = c.act(st1[:], st1[:], AF.Exp, waits=[t_ln], scale=0.5)
        t_rc = c.vrecip(st1[:], st1[:], waits=[t_ex])
        t_nb = c.vts(nbias[:], mtile[:], st1[:, 0:1], ALU.mult, waits=[t_rc],
                     s2=-1.0, op1=ALU.mult)
        t_x1 = c.act(x1pt[:], u6[:], AF.Identity, waits=[t_nb, t_x1sp], scale=st1[:],
                     bias=nbias[:])
        t_x1sp = c.dma("gpsimd", D["x1_d"][i], x1pt[:], waits=[t_x1])
        if lvl == "3":
            t_prev = t_x1sp
            continue
        t_last_lo = None
        for g in range(4):
            pb = trp6[g % 2]
            tlast = None
            for k in range(4):
                kt = g * 4 + k
                tlast = c.tr(pb[:, k * 128:(k + 1) * 128],
                             x1pt[:, kt * 128:(kt + 1) * 128],
                             waits=[t_x1, pf[g % 2]])
            for k in range(4):
                kt = g * 4 + k
                dsth = x1h[:, kt * T + i * 128: kt * T + i * 128 + 128]
                dstl = x1l[:, kt * T + i * 128: kt * T + i * 128 + 128]
                t_h = c.vcopy(dsth, pb[:, k * 128:(k + 1) * 128], waits=[tlast])
                t_last_lo = c.vtt(dstl, pb[:, k * 128:(k + 1) * 128], dsth, ALU.subtract,
                                  waits=[t_h])
            pf[g % 2] = t_last_lo
        t_prev = t_last_lo
    c.t_x1 = t_prev
    c.x1h, c.x1l = x1h, x1l
    p.barrier()
    es6b.close()


def _p7(c):
    nc, p, es, D = c.nc, c.p, c.es, c.D
    es7 = contextlib.ExitStack()
    sb7 = lambda n, s, d=F32: es7.enter_context(nc.sbuf_tensor(n, s, d))
    wf7 = sb7("wf7", [128, KT_IN * 128])
    ws7 = sb7("ws7", [128, KT_IN * 128], BF16)
    wfd = sb7("wfd", [128, DM])
    wsd = sb7("wsd", [128, 4 * DM], BF16)
    hidh = sb7("hidh", [128, 4 * 512], BF16)
    hidl = sb7("hidl", [128, 4 * 512], BF16)
    rscr = sb7("rscr", [128, 512])
    acc = sb7("acc", [128, 16 * 512])
    x1rl = sb7("x1rl", [128, DM])
    u7 = sb7("u7", [128, DM])
    outb = sb7("outb", [128, DM])
    sq7 = outb
    st1 = sb7("st1b", [128, 1])
    st2 = sb7("st2b", [128, 1])
    mtile = sb7("mtileb", [128, 1])
    nbias = sb7("nbiasb", [128, 1])
    up7 = [es7.enter_context(nc.psum_tensor(f"up7{i}", [128, 512], F32)) for i in range(2)]
    dn7 = [es7.enter_context(nc.psum_tensor(f"dn7{i}", [128, 512], F32)) for i in range(2)]
    tr7 = [es7.enter_context(nc.psum_tensor(f"tr7{i}", [128, 512], F32)) for i in range(2)]

    upT_re = D["upT"].rearrange("(k p) o -> p k o", p=128)
    downT_re = D["downT"].rearrange("(k p) o -> p k o", p=128)

    t_out_final = []
    t_half_prev = None
    for hb in range(2):
        tok0 = hb * 512
        t_accev = {}
        t_wldu = t_wsgu = None
        t_hid = {}
        t_hidh = {}
        t_wldd = t_wsgd = None
        for sl in range(16):
            for u in range(4):
                stp = sl * 4 + u
                t_wldu = c.dma("sync", wf7.ap().rearrange("p (k o) -> p k o", o=128),
                               upT_re[:, :, stp * 128:(stp + 1) * 128], waits=[t_wsgu])
                t_wsgu = c.sign(ws7[:], wf7[:], waits=[t_wldu, t_hid.get(stp - 1)])
                pb = up7[u % 2]
                nmm = 0
                for kt in range(KT_IN):
                    lw = ws7[:, kt * 128:(kt + 1) * 128]
                    for rh in (c.x1h[:, kt * T + tok0: kt * T + tok0 + 512],
                               c.x1l[:, kt * T + tok0: kt * T + tok0 + 512]):
                        nmm += 1
                        t_mmu = c.mm(pb[:], lw, rh, start=(nmm == 1), stop=(nmm == 2 * KT_IN),
                                     waits=[t_wsgu, c.t_x1, t_hid.get(stp - 2)]
                                     if nmm == 1 else (),
                                     inc="pT" if nmm == 2 * KT_IN else None)
                t_rl = c.act(rscr[:], pb[:], AF.Relu, waits=[t_mmu, t_hid.get(stp - 2),
                                                             c.tC[7], t_half_prev],
                             scale=c.upsc_s[:, stp:stp + 1])
                t_hh = c.act(hidh[:, u * 512:(u + 1) * 512], rscr[:], AF.Copy, waits=[t_rl])
                t_hidh[stp] = t_hh
                t_hid[stp] = c.vtt(hidl[:, u * 512:(u + 1) * 512], rscr[:],
                                   hidh[:, u * 512:(u + 1) * 512], ALU.subtract,
                                   waits=[t_hh])
            t_sgn = {}
            for kd in range(4):
                kg = sl * 4 + kd
                t_wldd = c.dma("sync", wfd[:], downT_re[:, kg, :], waits=[t_sgn.get(kd - 1)])
                t_sgn[kd] = c.sign(wsd[:, kd * DM:(kd + 1) * DM], wfd[:], waits=[t_wldd])
            for st2_ in range(16):
                pb = dn7[st2_ % 2]
                nmm = 0
                for kd in range(4):
                    lw = wsd[:, kd * DM + st2_ * 128: kd * DM + st2_ * 128 + 128]
                    for rh in (hidh[:, kd * 512:(kd + 1) * 512],
                               hidl[:, kd * 512:(kd + 1) * 512]):
                        nmm += 1
                        t_mmd = c.mm(pb[:], lw, rh, start=(nmm == 1), stop=(nmm == 8),
                                     waits=[t_sgn[3], t_hid[sl * 4 + 3], t_hidh[sl * 4 + 3],
                                            t_accev.get((sl - 1, st2_)),
                                            t_accev.get((sl, st2_ - 2)), t_half_prev]
                                     if nmm == 1 else (),
                                     inc="pT" if nmm == 8 else None)
                asl = acc[:, st2_ * 512:(st2_ + 1) * 512]
                if sl == 0:
                    t_accev[(sl, st2_)] = c.vcopy(asl, pb[:], waits=[t_mmd])
                else:
                    t_accev[(sl, st2_)] = c.vtt(asl, pb[:], asl, ALU.add,
                                                waits=[t_mmd, t_accev[(sl - 1, st2_)]])
        t_sc = {}
        for st2_ in range(16):
            asl = acc[:, st2_ * 512:(st2_ + 1) * 512]
            t_sc[st2_] = c.vts(asl, asl, c.dn2[:, st2_:st2_ + 1], ALU.mult,
                               waits=[t_accev[(15, st2_)], c.tdn2])
        t_prev = None
        for ii in range(4):
            i = hb * 4 + ii
            t_xr = c.dma("sync", x1rl[:], D["x1_d"][i], waits=[t_prev])
            t_u = []
            pf = [t_prev, t_prev]
            for g in range(4):
                pb = tr7[g % 2]
                tlast = None
                for k in range(4):
                    stq = g * 4 + k
                    tlast = c.tr(pb[:, k * 128:(k + 1) * 128],
                                 acc[:, stq * 512 + ii * 128: stq * 512 + ii * 128 + 128],
                                 waits=[t_sc[stq], pf[g % 2]])
                t_u.append(c.vtt(u7[:, g * 512:(g + 1) * 512], pb[:],
                                 x1rl[:, g * 512:(g + 1) * 512], ALU.add,
                                 waits=[tlast, t_xr, t_prev]))
                pf[g % 2] = t_u[-1]
            t_r = c.vreduce(st1[:], u7[:], waits=t_u)
            t_q = c.act(sq7[:], u7[:], AF.Square, waits=t_u + [t_prev], accum=st2[:])
            t_m = c.vts(mtile[:], st1[:], 1.0 / DM, ALU.mult, waits=[t_r])
            t_msq = c.vtt(st1[:], mtile[:], mtile[:], ALU.mult, waits=[t_m])
            t_var = c.vstt(st1[:], st2[:], 1.0 / DM, ALU.mult, st1[:], ALU.subtract,
                           waits=[t_q, t_msq])
            t_ln = c.act(st1[:], st1[:], AF.Ln, waits=[t_var, c.teps], bias=c.eps1[:])
            t_ex = c.act(st1[:], st1[:], AF.Exp, waits=[t_ln], scale=0.5)
            t_rc = c.vrecip(st1[:], st1[:], waits=[t_ex])
            t_nb = c.vts(nbias[:], mtile[:], st1[:, 0:1], ALU.mult, waits=[t_rc],
                         s2=-1.0, op1=ALU.mult)
            t_o = c.act(outb[:], u7[:], AF.Identity, waits=[t_nb], scale=st1[:], bias=nbias[:])
            t_prev = c.dma("gpsimd", D["out_d"][i * 128:(i + 1) * 128, :], outb[:],
                           waits=[t_o])
            t_out_final.append(t_prev)
        t_half_prev = t_prev
    c.p.op("sync", None, waits=t_out_final)
    c.p.op("gpsimd", None, waits=t_out_final)
    es7.close()
    c.es67.close()


def _host_prep(inputs):
    f32 = lambda a: np.ascontiguousarray(a, dtype=np.float32)
    base = {}
    base["winT"] = f32(np.asarray(inputs["L0_attn_W_in"]).T)
    base["woutT"] = f32(np.asarray(inputs["L0_attn_W_out"]).T)
    base["upT"] = f32(np.asarray(inputs["L0_ffn_up_weight"]).T)
    base["downT"] = f32(np.asarray(inputs["L0_ffn_down_weight"]).T)
    cw = np.asarray(inputs["L0_attn_conv_w"])
    base["convw"] = f32(cw.reshape(S_XBC, 128, 4).transpose(1, 0, 2).reshape(128, S_XBC * 4))
    base["convb"] = f32(np.asarray(inputs["L0_attn_conv_b"]).reshape(S_XBC, 128).T)
    base["dtb"] = f32(np.asarray(inputs["L0_attn_dt_bias"]).reshape(64, 1))
    base["alog"] = f32(np.asarray(inputs["L0_attn_A_log"]).reshape(64, 1))
    base["drow"] = f32(np.repeat(np.asarray(inputs["L0_attn_D"]), 64).reshape(1, DS))
    base["rmsw"] = f32(np.asarray(inputs["L0_attn_rms_w"]).reshape(KT_IN, 128).T)
    base["normw"] = f32(np.asarray(inputs["L0_attn_norm_w"]).reshape(DS // 128, 128).T)
    base["upsc"] = f32(np.asarray(inputs["L0_ffn_up_scale"]).reshape(DH // 128, 128).T)
    base["dnsc"] = f32(np.asarray(inputs["L0_ffn_down_scale"]).reshape(DM // 128, 128).T)
    base["osc"] = f32(np.asarray(inputs["L0_ffn_output_scale"]).reshape(1, 1))
    base["idn"] = np.eye(128, dtype=np.float32)
    s_idx = np.arange(2 * 128)[:, None]
    l_idx = np.arange(CS)[None, :]
    base["maskneg"] = np.where(s_idx <= l_idx, 0.0, NEG).astype(np.float32)
    x_full = np.ascontiguousarray(np.asarray(inputs["L0_attn_x"], dtype=np.float32)[0])
    return base, x_full


def kernel(**inputs):
    base, x_full = _host_prep(inputs)
    nc = build_program()
    in_maps = []
    for i in range(NCORES):
        m = dict(base)
        m["x"] = np.ascontiguousarray(x_full[i * T:(i + 1) * T])
        hm = np.zeros((128, 8), np.float32)
        if i > 0:
            hm[:, i - 1] = 1.0
        sm = np.zeros((128, 8), np.float32)
        sm[:, :i] = 1.0
        m["hm"] = hm
        m["sm"] = sm
        in_maps.append(m)
    res = run_bass_kernel_spmd(nc, in_maps, core_ids=list(range(NCORES)))
    out = np.concatenate([np.asarray(r["out"]) for r in res.results], axis=0)
    return out.reshape(1, T * NCORES, DM).astype(np.float32)
